# version 02 (final)
# baseline (speedup 1.0000x reference)
import contextlib
import numpy as np
import ml_dtypes
import concourse.bass as bass
import concourse.mybir as mybir
from concourse.bass_utils import run_bass_kernel_spmd
from concourse.expressions_rust import RuntimeValue

F32 = mybir.dt.float32
BF16 = mybir.dt.bfloat16
I32 = mybir.dt.int32
I16 = mybir.dt.int16
AF = mybir.ActivationFunctionType
ALU = mybir.AluOpType
BF = ml_dtypes.bfloat16

D = 4096
HID = 11008
S = 4096
B = 2
H = 32
DH = 128
TPC = 1024
NKC = D // 128
NOC = 86
SCALE = DH ** -0.5
SLOPES = [2.0 ** (-0.25 * (h + 1)) for h in range(H)]
JW = 4992
NCORES = 8

W13_STARTS = [min(1408 * c, HID - 1408) for c in range(NCORES)]


def build_dequant():
    nc = bass.Bass("TRN2", target_bir_lowering=False)
    w1s = nc.dram_tensor("w1s", [1408 * 32, 64], I32, kind="ExternalInput").ap()
    s1s = nc.dram_tensor("s1s", [1408, 64], F32, kind="ExternalInput").ap()
    w3s = nc.dram_tensor("w3s", [1408 * 32, 64], I32, kind="ExternalInput").ap()
    s3s = nc.dram_tensor("s3s", [1408, 64], F32, kind="ExternalInput").ap()
    w2s = nc.dram_tensor("w2s", [512 * 86, 64], I32, kind="ExternalInput").ap()
    s2s = nc.dram_tensor("s2s", [512, 86 * 2], F32, kind="ExternalInput").ap()
    w1t = nc.dram_tensor("w1t", [D, 1408], BF16, kind="ExternalOutput").ap()
    w3t = nc.dram_tensor("w3t", [D, 1408], BF16, kind="ExternalOutput").ap()
    w2t = nc.dram_tensor("w2t", [HID, 512], BF16, kind="ExternalOutput").ap()

    jobs = []
    for t in range(11):
        jobs.append((w1s, s1s, w1t, t, 32))
        jobs.append((w3s, s3s, w3t, t, 32))
    for t in range(4):
        jobs.append((w2s, s2s, w2t, t, 86))
    ntp = [j[4] for j in jobs]
    tp_base = [sum(ntp[:i]) for i in range(len(jobs) + 1)]

    NBUF = 2
    with contextlib.ExitStack() as es:
        wbuf = [es.enter_context(nc.sbuf_tensor(f"wbuf{i}", [128, 86 * 64], I16))
                for i in range(NBUF)]
        sbuf_ = [es.enter_context(nc.sbuf_tensor(f"sbuf{i}", [128, 86 * 2], F32))
                 for i in range(NBUF)]
        lsbt = es.enter_context(nc.sbuf_tensor("lsbt", [128, 86 * 64], I16))
        msbt = es.enter_context(nc.sbuf_tensor("msbt", [128, 86 * 64], I16))
        obuf = [es.enter_context(nc.sbuf_tensor(f"obuf{i}", [128, 86 * 128], BF16))
                for i in range(NBUF)]
        tbuf = [es.enter_context(nc.sbuf_tensor(f"tbuf{i}", [128, 128], BF16))
                for i in range(2)]
        ident = es.enter_context(nc.sbuf_tensor("ident", [128, 128], BF16))
        ps_t = [es.enter_context(nc.psum_tensor(f"pst{i}", [128, 128], BF16))
                for i in range(2)]
        sLD = es.enter_context(nc.semaphore("sLD"))
        sL0 = es.enter_context(nc.semaphore("sL0"))
        sL1 = es.enter_context(nc.semaphore("sL1"))
        sV = es.enter_context(nc.semaphore("sV"))
        sTP = es.enter_context(nc.semaphore("sTP"))
        sE = es.enter_context(nc.semaphore("sE"))
        sS0 = es.enter_context(nc.semaphore("sS0"))
        sS1 = es.enter_context(nc.semaphore("sS1"))
        block = es.enter_context(nc.Block())

        n = len(jobs)

        @block.gpsimd
        def g_body(g):
            g.memset(ident[:, :], 1.0)
            g.affine_select(out=ident[:, :], in_=ident[:, :],
                            pattern=[[-1, 128]], base=0, channel_multiplier=1,
                            compare_op=ALU.is_equal, fill=0.0).then_inc(sLD, 16)
            for i, (wsrc, ssrc, dst, t, nj) in enumerate(jobs):
                if i >= NBUF:
                    g.wait_ge(sV, (i - NBUF + 1) * 3)
                b = i % NBUF
                sl = sL0 if b == 0 else sL1
                g.dma_start(out=wbuf[b][:, 0:nj * 64],
                            in_=wsrc.rearrange("(r j) c -> r (j c)", j=nj)[
                                t * 128:(t + 1) * 128, :]).then_inc(sl, 16)
                g.dma_start(out=sbuf_[b][:, 0:nj * 2],
                            in_=ssrc[t * 128:(t + 1) * 128, 0:nj * 2]).then_inc(sl, 16)

        @block.vector
        def v_body(v):
            for i, (wsrc, ssrc, dst, t, nj) in enumerate(jobs):
                b = i % NBUF
                v.wait_ge(sL0 if b == 0 else sL1, 32 * (i // 2 + 1))
                w = wbuf[b][:, 0:nj * 64]
                v.tensor_scalar(out=lsbt[:, 0:nj * 64], in0=w, scalar1=15,
                                scalar2=8, op0=ALU.bitwise_and,
                                op1=ALU.bitwise_xor)
                v.tensor_scalar(out=msbt[:, 0:nj * 64], in0=w, scalar1=7.5,
                                scalar2=1.0 / 16.0, op0=ALU.subtract,
                                op1=ALU.mult).then_inc(sV, 1)
                if i >= NBUF:
                    v.wait_ge(sTP, tp_base[i - NBUF + 1])
                ov = obuf[b][:, 0:nj * 128].rearrange(
                    "p (j par c) -> p j par c", par=2, c=64)
                sv = sbuf_[b][:, 0:nj * 2].rearrange("p (j par) -> p j par", par=2)
                v.scalar_tensor_tensor(
                    out=ov[:, :, 0, :], in0=msbt[:, 0:nj * 64].rearrange(
                        "p (j c) -> p j c", c=64), scalar=0.0,
                    in1=sv[:, :, 0:1].broadcast_to([128, nj, 64]),
                    op0=ALU.add, op1=ALU.mult).then_inc(sV, 1)
                v.scalar_tensor_tensor(
                    out=ov[:, :, 1, :], in0=lsbt[:, 0:nj * 64].rearrange(
                        "p (j c) -> p j c", c=64), scalar=8.0,
                    in1=sv[:, :, 1:2].broadcast_to([128, nj, 64]),
                    op0=ALU.subtract, op1=ALU.mult).then_inc(sV, 1)
                for k in range(nj):
                    g = tp_base[i] + k
                    v.wait_ge(sTP, g + 1)
                    if g >= 2:
                        v.wait_ge(sS0 if g % 2 == 0 else sS1, 16 * (g // 2))
                    v.tensor_copy(out=tbuf[g % 2][:, :],
                                  in_=ps_t[g % 2][:, :]).then_inc(sE, 1)

        @block.tensor
        def t_body(t):
            t.wait_ge(sLD, 16)
            for i, (wsrc, ssrc, dst, tt_, nj) in enumerate(jobs):
                b = i % NBUF
                v0 = 3 * (i + 1)
                t.wait_ge(sV, v0)
                for k in range(nj):
                    g = tp_base[i] + k
                    if g >= 2:
                        t.wait_ge(sE, g - 1)
                    t.transpose(out=ps_t[g % 2][:, :],
                                in_=obuf[b][:, k * 128:(k + 1) * 128],
                                identity=ident[:, :]).then_inc(sTP, 1)

        @block.sync
        def s_body(sy):
            for i, (wsrc, ssrc, dst, t, nj) in enumerate(jobs):
                for k in range(nj):
                    g = tp_base[i] + k
                    sy.wait_ge(sE, g + 1)
                    sy.dma_start(out=dst[k * 128:(k + 1) * 128,
                                         t * 128:(t + 1) * 128],
                                 in_=tbuf[g % 2][:, :]).then_inc(
                        sS0 if g % 2 == 0 else sS1, 16)
            ntot = tp_base[n]
            sy.wait_ge(sS0, 16 * ((ntot + 1) // 2))
            sy.wait_ge(sS1, 16 * (ntot // 2))

    return nc


_DEQUANT_NC = None


def _dequant_inputs(inputs):
    w1 = np.asarray(inputs["L0_ffn_w1"]).reshape(HID, 32, 64)
    s1 = np.asarray(inputs["L0_ffn_s1"]).reshape(HID, 64)
    w3 = np.asarray(inputs["L0_ffn_w3"]).reshape(HID, 32, 64)
    s3 = np.asarray(inputs["L0_ffn_s3"]).reshape(HID, 64)
    w2 = np.asarray(inputs["L0_ffn_w2"]).reshape(D, 86, 64)
    s2 = np.asarray(inputs["L0_ffn_s2"]).reshape(D, 172)
    maps = []
    for c in range(NCORES):
        r0 = W13_STARTS[c]
        maps.append(dict(
            w1s=np.ascontiguousarray(w1[r0:r0 + 1408]).reshape(-1, 64),
            s1s=np.ascontiguousarray(s1[r0:r0 + 1408, 0:64]),
            w3s=np.ascontiguousarray(w3[r0:r0 + 1408]).reshape(-1, 64),
            s3s=np.ascontiguousarray(s3[r0:r0 + 1408, 0:64]),
            w2s=np.ascontiguousarray(w2[512 * c:512 * (c + 1)]).reshape(-1, 64),
            s2s=np.ascontiguousarray(s2[512 * c:512 * (c + 1)]),
        ))
    return maps


def _assemble_transposed(res):
    w1t = np.zeros((D, HID), BF)
    w3t = np.zeros((D, HID), BF)
    w2t = np.zeros((HID, D), BF)
    for c in range(NCORES):
        r0 = W13_STARTS[c]
        w1t[:, r0:r0 + 1408] = res.results[c]["w1t"]
        w3t[:, r0:r0 + 1408] = res.results[c]["w3t"]
        w2t[:, 512 * c:512 * (c + 1)] = res.results[c]["w2t"]
    return w1t, w3t, w2t


def run_dequant(inputs):
    global _DEQUANT_NC
    if _DEQUANT_NC is None:
        _DEQUANT_NC = build_dequant()
    res = run_bass_kernel_spmd(_DEQUANT_NC, _dequant_inputs(inputs),
                               core_ids=list(range(NCORES)))
    return _assemble_transposed(res)


def kernel(**inputs):
    w1t, w3t, w2t = run_dequant(inputs)
    res1 = run_main1(inputs, w1t, w3t, w2t)
    res2 = run_main2(inputs, res1)
    out = np.zeros((B, S, D), np.float32)
    for c in range(NCORES):
        b, r = c // 4, c % 4
        out[b, r * TPC:(r + 1) * TPC, :] = res2.results[c]["out_tm"]
    return out


class Prog:

    def __init__(self, nc, es):
        self.nc = nc
        self.es = es
        self.sems = {}
        self.cnt = {}

    def sem(self, name):
        if name not in self.sems:
            self.sems[name] = self.es.enter_context(self.nc.semaphore(name))
            self.cnt[name] = 0
        return self.sems[name]

    def inc(self, name, k=1):
        h = self.sem(name)
        self.cnt[name] += k
        return h, self.cnt[name]

    def sb(self, name, shape, dtype):
        return self.es.enter_context(self.nc.sbuf_tensor(name, shape, dtype))

    def ps(self, name, shape, dtype=F32):
        return self.es.enter_context(self.nc.psum_tensor(name, shape, dtype))


def emit_init_phase(p, block, ones_col, ones_row, ident, dm, tmp_i16, tmpf):
    nc = p.nc
    p.sem("sINIT")
    p.sem("sINIg")
    p.sem("sINIv")
    p.sem("sINIb")
    ph = contextlib.ExitStack()
    tqb_i = ph.enter_context(nc.sbuf_tensor("tqbi", [1, 1], I32))
    tqb_f = ph.enter_context(nc.sbuf_tensor("tqbf", [1, 1], BF16))
    tqb_col = ph.enter_context(nc.sbuf_tensor("tqbc", [128, 1], F32))
    ps_b = ph.enter_context(nc.psum_tensor("tqbp", [128, 1], F32))

    def g_body(g):
        g.memset(ones_col[:, :], 1.0)
        g.memset(ones_row[:, :], 1.0)
        g.memset(ident[:, :], 1.0)
        g.affine_select(out=ident[:, :], in_=ident[:, :], pattern=[[-1, 128]],
                        base=0, channel_multiplier=1,
                        compare_op=ALU.is_equal, fill=0.0)
        g.iota(out=tmp_i16[:, :], pattern=[[-1, JW]], base=3968,
               channel_multiplier=1)
        h, _ = p.inc("sINIg")
        g.tensor_copy(out=dm[:, :], in_=tmp_i16[:, :]).then_inc(h, 1)

    block.gpsimd(g_body)

    def v1_body(v):
        with v.register("tqr") as tqr:
            pid = v.partition_id()
            v.reg_alu(tqr, pid, 4, ALU.mod)
            v.reg_alu(tqr, tqr, 1024, ALU.mult)
            v.reg_save(tqb_i[0:1, 0:1], tqr)
        h, _ = p.inc("sINIv")
        v.tensor_copy(out=tqb_f[:, :], in_=tqb_i[:, :]).then_inc(h, 1)

    block.vector(v1_body)

    def t_body(t):
        t.wait_ge(p.sems["sINIv"], p.cnt["sINIv"])
        h, _ = p.inc("sINIb")
        t.matmul(out=ps_b[:, :], lhsT=ones_row[:, :], rhs=tqb_f[:, :],
                 start=True, stop=True).then_inc(h, 1)

    block.tensor(t_body)

    def v2_body(v):
        v.wait_ge(p.sems["sINIb"], p.cnt["sINIb"])
        v.tensor_copy(out=tqb_col[:, :], in_=ps_b[:, :])
        v.wait_ge(p.sems["sINIg"], p.cnt["sINIg"])
        v.scalar_tensor_tensor(
            out=dm[:, :], in0=dm[:, :], scalar=1.0,
            in1=tqb_col[:, 0:1].broadcast_to([128, JW]),
            op0=ALU.mult, op1=ALU.subtract)
        v.tensor_scalar(out=tmpf[:, :], in0=dm[:, :], scalar1=0.0,
                        scalar2=None, op0=ALU.is_gt)
        h, _ = p.inc("sINIT")
        v.scalar_tensor_tensor(out=dm[:, :], in0=tmpf[:, :], scalar=-1e30,
                               in1=dm[:, :], op0=ALU.mult,
                               op1=ALU.add).then_inc(h, 1)

    block.vector(v2_body)
    ph.close()
    return ("sINIT", p.cnt["sINIT"])


def emit_attention(p, block, *, stream, kf, vtm, attn_out, dm, masterh,
                   ones_col, ones_row, wq_dram, wq_col0, wq_sb, tag=""):
    nc = p.nc
    sW = "aw" + tag
    sW1 = "ax" + tag
    sQ = "aq" + tag
    sA = "aa" + tag
    sE = "ae" + tag
    sM = "am" + tag
    sS = "as" + tag
    sG = "ag" + tag
    sB = "ab" + tag
    sR = "ar" + tag
    sO = "ao" + tag
    sH = "mh" + tag
    for s in (sW, sW1, sQ, sA, sE, sM, sS, sG, sB, sR, sO, sH):
        p.sem(s)
    DEPTH = 3
    NJ = 2 * H
    wq_r = wq_dram.rearrange("(dc p) n -> p dc n", p=128)

    ph = contextlib.ExitStack()
    q_sb = ph.enter_context(nc.sbuf_tensor(f"qsb{tag}", [128, 512], BF16))
    pE = [ph.enter_context(nc.sbuf_tensor(f"pE{tag}{i}", [128, 512], BF16))
          for i in range(DEPTH)]
    pM = [ph.enter_context(nc.sbuf_tensor(f"pM{tag}{i}", [128, 512], BF16))
          for i in range(DEPTH)]
    rrow = ph.enter_context(nc.sbuf_tensor(f"rrow{tag}", [1, 512], F32))
    rrow_bf = ph.enter_context(nc.sbuf_tensor(f"rrowb{tag}", [1, 512], BF16))
    rrep = ph.enter_context(nc.sbuf_tensor(f"rrep{tag}", [128, 512], BF16))
    ps_s = [ph.enter_context(nc.psum_tensor(f"pss{tag}{i}", [128, 512], F32))
            for i in range(DEPTH)]
    ps_q = ph.enter_context(nc.psum_tensor(f"psq{tag}", [128, 512], F32))
    ps_sum = ph.enter_context(nc.psum_tensor(f"pssum{tag}", [1, 512], F32))
    ps_av = ph.enter_context(nc.psum_tensor(f"psav{tag}", [128, 512], F32))
    ps_bc = ph.enter_context(nc.psum_tensor(f"psbc{tag}", [128, 512], F32))

    base = {s: p.cnt[s] for s in (sW, sW1, sQ, sA, sE, sM, sS, sG, sB, sR,
                                  sO, sH)}
    sWs = [sW, sW1]

    def g_body(g):
        for h in range(H):
            if h >= 2:
                g.wait_ge(p.sems[sQ], base[sQ] + (h - 1) * 2)
            g.dma_start(
                out=wq_sb[h % 2][:, :],
                in_=wq_r[:, :, wq_col0 + h * 128: wq_col0 + (h + 1) * 128],
            ).then_inc(p.sems[sWs[h % 2]], 16)

    block.gpsimd(g_body)
    p.cnt[sW] += 16 * (H // 2)
    p.cnt[sW1] += 16 * (H // 2)

    def a_body(a):
        for h in range(H):
            if h >= 1:
                a.wait_ge(p.sems[sM], base[sM] + 64 * h)
            a.activation(out=masterh[:, :], in_=dm[:, :], func=AF.Exp,
                         scale=SLOPES[h]).then_inc(p.sems[sH], 1)
            for half in range(2):
                j = h * 2 + half
                a.wait_ge(p.sems[sQ], base[sQ] + j + 1)
                a.copy(out=q_sb[:, :], in_=ps_q[:, :]).then_inc(p.sems[sA], 1)
                for kc in range(NKC):
                    i = j * NKC + kc
                    a.wait_ge(p.sems[sS], base[sS] + i + 1)
                    if i >= DEPTH:
                        a.wait_ge(p.sems[sM], base[sM] + i - DEPTH + 1)
                    a.activation(out=pE[i % DEPTH][:, :],
                                 in_=ps_s[i % DEPTH][:, :], func=AF.Exp,
                                 scale=SCALE).then_inc(p.sems[sE], 1)

    block.scalar(a_body)
    p.cnt[sH] += H
    p.cnt[sA] += NJ
    p.cnt[sE] += NJ * NKC

    def v_body(v):
        for h in range(H):
            for half in range(2):
                j = h * 2 + half
                for kc in range(NKC):
                    i = j * NKC + kc
                    v.wait_ge(p.sems[sE], base[sE] + i + 1)
                    if i >= DEPTH:
                        v.wait_ge(p.sems[sS], base[sS] + i - DEPTH + 2)
                    off = 3968 + 512 * half - 128 * kc
                    v.tensor_tensor(
                        out=pM[i % DEPTH][:, :], in0=pE[i % DEPTH][:, :],
                        in1=masterh[:, off:off + 512],
                        op=ALU.mult).then_inc(p.sems[sM], 1)
                v.wait_ge(p.sems[sG], base[sG] + j + 1)
                v.reciprocal(out=rrow[:, :], in_=ps_sum[:, :])
                v.tensor_copy(out=rrow_bf[:, :],
                              in_=rrow[:, :]).then_inc(p.sems[sR], 1)
                v.wait_ge(p.sems[sB], base[sB] + j + 1)
                v.tensor_copy(out=rrep[:, :], in_=ps_bc[:, :])
                v.tensor_tensor(
                    out=attn_out[:, j * 512: j * 512 + 512],
                    in0=ps_av[:, :], in1=rrep[:, :],
                    op=ALU.mult).then_inc(p.sems[sO], 1)

    block.vector(v_body)
    p.cnt[sM] += NJ * NKC
    p.cnt[sR] += NJ
    p.cnt[sO] += NJ

    def t_body(t):
        for h in range(H):
            for half in range(2):
                j = h * 2 + half
                if half == 0:
                    t.wait_ge(p.sems[sWs[h % 2]],
                              base[sWs[h % 2]] + 16 * (h // 2 + 1))
                for dc in range(NKC):
                    mm = t.matmul(
                        out=ps_q[:, :],
                        lhsT=wq_sb[h % 2][:, dc * 128:(dc + 1) * 128],
                        rhs=stream[:, dc * 1024 + half * 512:
                                   dc * 1024 + half * 512 + 512],
                        start=(dc == 0), stop=(dc == NKC - 1))
                mm.then_inc(p.sems[sQ], 1)
                t.wait_ge(p.sems[sA], base[sA] + j + 1)
                for kc in range(NKC):
                    i = j * NKC + kc
                    if i >= DEPTH:
                        t.wait_ge(p.sems[sE], base[sE] + i - DEPTH + 1)
                    t.matmul(out=ps_s[i % DEPTH][:, :],
                             lhsT=kf[:, kc * 128:(kc + 1) * 128],
                             rhs=q_sb[:, :],
                             start=True, stop=True).then_inc(p.sems[sS], 1)
                    t.wait_ge(p.sems[sM], base[sM] + i + 1)
                    t.matmul(out=ps_sum[:, :], lhsT=ones_col[:, :],
                             rhs=pM[i % DEPTH][:, :],
                             start=(kc == 0), stop=(kc == NKC - 1))
                    mm = t.matmul(out=ps_av[:, :],
                                  lhsT=vtm[:, kc * 128:(kc + 1) * 128],
                                  rhs=pM[i % DEPTH][:, :],
                                  start=(kc == 0), stop=(kc == NKC - 1))
                mm.then_inc(p.sems[sG], 1)
                t.wait_ge(p.sems[sR], base[sR] + j + 1)
                t.matmul(out=ps_bc[:, :], lhsT=ones_row[:, :],
                         rhs=rrow_bf[:, :], start=True,
                         stop=True).then_inc(p.sems[sB], 1)
                t.wait_ge(p.sems[sO], base[sO] + j + 1)

    block.tensor(t_body)
    p.cnt[sQ] += NJ
    p.cnt[sS] += NJ * NKC
    p.cnt[sG] += NJ
    p.cnt[sB] += NJ
    ph.close()
    return ("ao" + tag, p.cnt[sO])


def emit_fence(p, block, waits):
    def mk(e):
        for s, v in waits:
            e.wait_ge(p.sems[s], v)
    block.sync(mk)
    block.gpsimd(mk)
    block.scalar(mk)
    block.vector(mk)
    block.tensor(mk)


def emit_stream_load(p, block, stream, x0t_own):
    s = "xl"
    p.sem(s)

    def g_body(g):
        for dc in range(NKC):
            g.dma_start(out=stream[:, dc * 1024:(dc + 1) * 1024],
                        in_=x0t_own[dc * 128:(dc + 1) * 128, :]).then_inc(
                            p.sems[s], 16)

    block.gpsimd(g_body)
    p.cnt[s] += 16 * NKC
    return p.cnt[s]


def emit_l0kv(p, block, *, x0t_batch, wkv_dram, kf, vtm, ident, wbuf):
    nc = p.nc
    sL = "kvw"
    sXB = ["kvx0", "kvx1", "kvx2"]
    sP = "kvp"
    sV = "kvv"
    sT = "kvt"
    for s in (sL, sP, sV, sT, *sXB):
        p.sem(s)
    base = {s: p.cnt[s] for s in (sL, sP, sV, sT, *sXB)}
    xr = x0t_batch.rearrange("(dc p) t -> dc p t", p=128)
    wr = wkv_dram.rearrange("(dc p) n -> p dc n", p=128)

    with contextlib.ExitStack() as ph:
        xb = [ph.enter_context(nc.sbuf_tensor(f"kvxb{i}", [128, 512], BF16))
              for i in range(3)]
        ktmp = ph.enter_context(nc.sbuf_tensor("ktmp", [128, 128], BF16))
        ps_kv = [ph.enter_context(nc.psum_tensor(f"pskv{i}", [128, 512], F32))
                 for i in range(4)]
        ps_t = ph.enter_context(nc.psum_tensor("pskt", [128, 128], BF16))

        def g_body(g):
            g.dma_start(out=wbuf[:, :], in_=wr[:, :, :]).then_inc(p.sems[sL], 16)
            for span in range(8):
                for dc in range(NKC):
                    i = span * NKC + dc
                    if i >= 3:
                        g.wait_ge(p.sems[sP], base[sP] + (i - 3) * 4 + 4)
                    g.dma_start(
                        out=xb[i % 3][:, :],
                        in_=xr[dc, :, span * 512:(span + 1) * 512],
                    ).then_inc(p.sems[sXB[i % 3]], 16)

        block.gpsimd(g_body)
        p.cnt[sL] += 16
        for i in range(8 * NKC):
            p.cnt[sXB[i % 3]] += 16

        def t_body(t):
            t.wait_ge(p.sems[sL], base[sL] + 16)
            for span in range(8):
                for dc in range(NKC):
                    i = span * NKC + dc
                    t.wait_ge(p.sems[sXB[i % 3]],
                              base[sXB[i % 3]] + 16 * (i // 3 + 1))
                    if dc == 0 and span > 0:
                        t.wait_ge(p.sems[sV], base[sV] + span * 8)
                    for q in range(4):
                        t.matmul(out=ps_kv[q][:, 0:256],
                                 lhsT=xb[i % 3][:, q * 128:(q + 1) * 128],
                                 rhs=wbuf[:, dc * 256:(dc + 1) * 256],
                                 start=(dc == 0),
                                 stop=(dc == NKC - 1)).then_inc(p.sems[sP], 1)
                for q in range(4):
                    tkc = span * 4 + q
                    t.wait_ge(p.sems[sV], base[sV] + span * 8 + q * 2 + 1)
                    t.transpose(out=ps_t[:, :], in_=ktmp[:, :],
                                identity=ident[:, :]).then_inc(p.sems[sT], 1)
                    t.wait_ge(p.sems[sV], base[sV] + span * 8 + q * 2 + 2)

        block.tensor(t_body)
        p.cnt[sP] += 8 * NKC * 4
        p.cnt[sT] += 32

        def v_body(v):
            for span in range(8):
                for q in range(4):
                    tkc = span * 4 + q
                    v.wait_ge(p.sems[sP], base[sP] + (span * NKC + NKC - 1) * 4 + q + 1)
                    v.tensor_copy(out=vtm[:, tkc * 128:(tkc + 1) * 128],
                                  in_=ps_kv[q][:, 128:256])
                    v.tensor_copy(out=ktmp[:, :],
                                  in_=ps_kv[q][:, 0:128]).then_inc(p.sems[sV], 1)
                    v.wait_ge(p.sems[sT], base[sT] + tkc + 1)
                    v.tensor_copy(out=kf[:, tkc * 128:(tkc + 1) * 128],
                                  in_=ps_t[:, :]).then_inc(p.sems[sV], 1)

        block.vector(v_body)
        p.cnt[sV] += 64
    return dict(done=("kvv", p.cnt[sV]))


def emit_ln_finalize(p, block, *, ps_sz, ps_sq, stream, half, ones_row, tag,
                     stats_wait, ln_scratch):
    nc = p.nc
    sLV, sLA, sLB, sLC, sLN = (x + tag for x in ("lv", "la", "lb", "lc", "ln"))
    for s in (sLV, sLA, sLB, sLC, sLN):
        p.sem(s)
    base = {s: p.cnt[s] for s in (sLV, sLA, sLB, sLC, sLN)}

    with contextlib.ExitStack() as ph:
        murow = ln_scratch["murow"]
        msrow = ln_scratch["msrow"]
        varrow = ln_scratch["varrow"]
        mu_bf = ln_scratch["mu_bf"]
        rstd_bf = ln_scratch["rstd_bf"]
        mu_rep = ln_scratch["mu_rep"]
        rstd_rep = ln_scratch["rstd_rep"]
        ps_bc = ph.enter_context(nc.psum_tensor("psbcl" + tag, [128, 512], F32))

        def v1(v):
            v.wait_ge(p.sems[stats_wait[0]], stats_wait[1])
            v.tensor_scalar(out=murow[:, :], in0=ps_sz[:, :], scalar1=1.0 / D,
                            scalar2=None, op0=ALU.mult)
            v.tensor_scalar(out=msrow[:, :], in0=ps_sq[:, :], scalar1=1.0 / D,
                            scalar2=None, op0=ALU.mult)
            v.tensor_copy(out=mu_bf[:, :], in_=murow[:, :])
            v.tensor_tensor(out=murow[:, :], in0=murow[:, :], in1=murow[:, :],
                            op=ALU.mult)
            v.tensor_tensor(out=varrow[:, :], in0=msrow[:, :], in1=murow[:, :],
                            op=ALU.subtract)
            v.tensor_scalar(out=varrow[:, :], in0=varrow[:, :], scalar1=1e-5,
                            scalar2=None, op0=ALU.add).then_inc(p.sems[sLV], 1)

        block.vector(v1)
        p.cnt[sLV] += 1

        def a1(a):
            a.wait_ge(p.sems[sLV], base[sLV] + 1)
            a.activation(out=varrow[:, :], in_=varrow[:, :], func=AF.Ln)
            a.activation(out=rstd_bf[:, :], in_=varrow[:, :], func=AF.Exp,
                         scale=-0.5).then_inc(p.sems[sLA], 1)

        block.scalar(a1)
        p.cnt[sLA] += 1

        def t1(t):
            t.wait_ge(p.sems[sLV], base[sLV] + 1)
            t.matmul(out=ps_bc[:, :], lhsT=ones_row[:, :], rhs=mu_bf[:, :],
                     start=True, stop=True).then_inc(p.sems[sLB], 1)
            t.wait_ge(p.sems[sLA], base[sLA] + 1)
            t.wait_ge(p.sems[sLC], base[sLC] + 1)
            t.matmul(out=ps_bc[:, :], lhsT=ones_row[:, :], rhs=rstd_bf[:, :],
                     start=True, stop=True).then_inc(p.sems[sLB], 1)

        block.tensor(t1)
        p.cnt[sLB] += 2

        def v2(v):
            v.wait_ge(p.sems[sLB], base[sLB] + 1)
            v.tensor_copy(out=mu_rep[:, :], in_=ps_bc[:, :]).then_inc(
                p.sems[sLC], 1)
            v.wait_ge(p.sems[sLB], base[sLB] + 2)
            v.tensor_copy(out=rstd_rep[:, :], in_=ps_bc[:, :])
            for dc in range(NKC):
                sl = stream[:, dc * 1024 + half * 512: dc * 1024 + half * 512 + 512]
                v.tensor_tensor(out=sl, in0=sl, in1=mu_rep[:, :],
                                op=ALU.subtract)
                mm = v.tensor_tensor(out=sl, in0=sl, in1=rstd_rep[:, :],
                                     op=ALU.mult)
            mm.then_inc(p.sems[sLN], 1)

        block.vector(v2)
        p.cnt[sLC] += 1
        p.cnt[sLN] += 1
    return (sLN, p.cnt[sLN])


def emit_outproj_ln(p, block, *, stream, rhs, w_dram, w_col0, wbufs, ones_col,
                    ones_row, tag, ln_scratch, bias_sb=None, extra_mms=None,
                    extra_wait=None):
    nc = p.nc
    sWL, sZO, sZE, sSQ, sZS = (x + tag for x in ("wl", "zo", "ze", "sq", "zs"))
    sWL1 = "wm" + tag
    for s in (sWL, sWL1, sZO, sZE, sSQ, sZS):
        p.sem(s)
    base = {s: p.cnt[s] for s in (sWL, sWL1, sZO, sZE, sSQ, sZS)}
    sWLs = [sWL, sWL1]
    wr = w_dram.rearrange("(dc p) n -> p dc n", p=128)

    with contextlib.ExitStack() as ph:
        sq_sb = [ph.enter_context(nc.sbuf_tensor(f"sqsb{tag}{i}", [128, 512], BF16))
                 for i in range(2)]
        ps_zo = [ph.enter_context(nc.psum_tensor(f"pszo{tag}{i}", [128, 512], F32))
                 for i in range(3)]
        ps_sz = [ph.enter_context(nc.psum_tensor(f"pssz{tag}{i}", [1, 512], F32))
                 for i in range(2)]
        ps_sq = [ph.enter_context(nc.psum_tensor(f"pssq{tag}{i}", [1, 512], F32))
                 for i in range(2)]

        def g_body(g):
            if extra_wait is not None:
                g.wait_ge(p.sems[extra_wait[0]], extra_wait[1])
            for dpc in range(NKC):
                if dpc >= 2:
                    g.wait_ge(p.sems[sZO], base[sZO] + (dpc - 1) * 2)
                g.dma_start(
                    out=wbufs[dpc % 2][:, :],
                    in_=wr[:, :, w_col0 + dpc * 128: w_col0 + (dpc + 1) * 128],
                ).then_inc(p.sems[sWLs[dpc % 2]], 16)

        block.gpsimd(g_body)
        p.cnt[sWL] += 16 * (NKC // 2)
        p.cnt[sWL1] += 16 * (NKC // 2)

        def t_body(t):
            for dpc in range(NKC):
                t.wait_ge(p.sems[sWLs[dpc % 2]],
                          base[sWLs[dpc % 2]] + 16 * (dpc // 2 + 1))
                for half in range(2):
                    k = dpc * 2 + half
                    ps = ps_zo[k % 3]
                    if k >= 3:
                        t.wait_ge(p.sems[sZE], base[sZE] + k - 2)
                    for dcc in range(NKC):
                        mm = t.matmul(
                            out=ps[:, :],
                            lhsT=wbufs[dpc % 2][:, dcc * 128:(dcc + 1) * 128],
                            rhs=rhs[:, dcc * 1024 + half * 512:
                                    dcc * 1024 + half * 512 + 512],
                            start=(dcc == 0),
                            stop=(dcc == NKC - 1 and extra_mms is None))
                    if extra_mms is not None:
                        mm = extra_mms(t, dpc, half, ps)
                    mm.then_inc(p.sems[sZO], 1)
                if dpc >= 1:
                    for half in range(2):
                        kk = (dpc - 1) * 2 + half
                        t.wait_ge(p.sems[sZE], base[sZE] + kk + 1)
                        t.wait_ge(p.sems[sSQ], base[sSQ] + kk + 1)
                        sl = stream[:, (dpc - 1) * 1024 + half * 512:
                                    (dpc - 1) * 1024 + half * 512 + 512]
                        t.matmul(out=ps_sz[half][:, :], lhsT=ones_col[:, :],
                                 rhs=sl, start=(dpc - 1 == 0), stop=False)
                        t.matmul(out=ps_sq[half][:, :], lhsT=ones_col[:, :],
                                 rhs=sq_sb[half][:, :],
                                 start=(dpc - 1 == 0),
                                 stop=False).then_inc(p.sems[sZS], 1)
            for half in range(2):
                kk = (NKC - 1) * 2 + half
                t.wait_ge(p.sems[sZE], base[sZE] + kk + 1)
                t.wait_ge(p.sems[sSQ], base[sSQ] + kk + 1)
                sl = stream[:, (NKC - 1) * 1024 + half * 512:
                            (NKC - 1) * 1024 + half * 512 + 512]
                t.matmul(out=ps_sz[half][:, :], lhsT=ones_col[:, :],
                         rhs=sl, start=False, stop=True)
                t.matmul(out=ps_sq[half][:, :], lhsT=ones_col[:, :],
                         rhs=sq_sb[half][:, :], start=False,
                         stop=True).then_inc(p.sems[sZS], 1)

        block.tensor(t_body)
        p.cnt[sZO] += 2 * NKC
        p.cnt[sZS] += 2 * NKC

        def v_body(v):
            for dpc in range(NKC):
                for half in range(2):
                    k = dpc * 2 + half
                    v.wait_ge(p.sems[sZO], base[sZO] + k + 1)
                    sl = stream[:, dpc * 1024 + half * 512:
                                dpc * 1024 + half * 512 + 512]
                    if bias_sb is not None:
                        v.scalar_tensor_tensor(
                            out=sl, in0=ps_zo[k % 3][:, :],
                            scalar=bias_sb[:, dpc:dpc + 1], in1=sl,
                            op0=ALU.add, op1=ALU.add).then_inc(p.sems[sZE], 1)
                    else:
                        v.tensor_tensor(out=sl, in0=ps_zo[k % 3][:, :],
                                        in1=sl, op=ALU.add).then_inc(
                                            p.sems[sZE], 1)

        block.vector(v_body)
        p.cnt[sZE] += 2 * NKC

        def a_body(a):
            for dpc in range(NKC):
                for half in range(2):
                    k = dpc * 2 + half
                    a.wait_ge(p.sems[sZE], base[sZE] + k + 1)
                    if k >= 2:
                        a.wait_ge(p.sems[sZS], base[sZS] + k - 1)
                    sl = stream[:, dpc * 1024 + half * 512:
                                dpc * 1024 + half * 512 + 512]
                    a.activation(out=sq_sb[half][:, :], in_=sl,
                                 func=AF.Square).then_inc(p.sems[sSQ], 1)

        block.scalar(a_body)
        p.cnt[sSQ] += 2 * NKC

        dones = []
        for half in range(2):
            dn = emit_ln_finalize(
                p, block, ps_sz=ps_sz[half], ps_sq=ps_sq[half], stream=stream,
                half=half, ones_row=ones_row, tag=tag + f"h{half}",
                stats_wait=(sZS, base[sZS] + 2 * NKC), ln_scratch=ln_scratch)
            dones.append(dn)
    return dones


def emit_ffn(p, block, *, stream, h_sb, w1t, w3t, w2t, ones_col, ones_row,
             w13bufs, w2bufs, ln_scratch, tag="f"):
    nc = p.nc
    dones = []
    prev_f2 = None
    for half in range(2):
        tg = f"{tag}{half}"
        s1L, s1G, s1E, s1H = (x + tg for x in ("f1l", "f1g", "f1e", "f1h"))
        s2L, s2G, sZE, sSQ, sZS = (x + tg for x in
                                   ("f2l", "f2g", "fze", "fsq", "fzs"))
        s1L1, s2L1 = "f1m" + tg, "f2m" + tg
        for s in (s1L, s1L1, s1G, s1E, s1H, s2L, s2L1, s2G, sZE, sSQ, sZS):
            p.sem(s)
        base = {s: p.cnt[s] for s in (s1L, s1L1, s1G, s1E, s1H, s2L, s2L1,
                                      s2G, sZE, sSQ, sZS)}
        s1Ls = [s1L, s1L1]
        s2Ls = [s2L, s2L1]

        with contextlib.ExitStack() as ph:
            siluA = [ph.enter_context(
                nc.sbuf_tensor(f"silu{tg}{i}", [128, 512], BF16))
                for i in range(2)]
            ps_a = [ph.enter_context(
                nc.psum_tensor(f"psa{tg}{i}", [128, 512], F32))
                for i in range(2)]
            ps_b = [ph.enter_context(
                nc.psum_tensor(f"psb{tg}{i}", [128, 512], F32))
                for i in range(2)]

            def g1(g, prev_f2=prev_f2):
                if prev_f2 is not None:
                    g.wait_ge(p.sems[prev_f2[0]], prev_f2[1])
                for ob in range(43):
                    for gdc in range(4):
                        li = ob * 4 + gdc
                        if li >= 2:
                            g.wait_ge(p.sems[s1G], base[s1G] + li - 1)
                        for mi, wt in enumerate((w1t, w3t)):
                            g.dma_start(
                                out=w13bufs[li % 2][:, mi * 2048:
                                                    (mi + 1) * 2048].rearrange(
                                    "p (dc c) -> p dc c", dc=8),
                                in_=wt.rearrange("(gd dc p) o -> p gd dc o",
                                                 p=128, dc=8)[
                                    :, gdc, :, ob * 256:(ob + 1) * 256],
                            ).then_inc(p.sems[s1Ls[li % 2]], 16)

            block.sync(g1)
            p.cnt[s1L] += 16 * 43 * 2 * 2
            p.cnt[s1L1] += 16 * 43 * 2 * 2

            def t1(t):
                for ob in range(43):
                    if ob >= 1:
                        t.wait_ge(p.sems[s1H], base[s1H] + (ob - 1) * 2 + 2)
                    for gdc in range(4):
                        li = ob * 4 + gdc
                        t.wait_ge(p.sems[s1Ls[li % 2]],
                                  base[s1Ls[li % 2]] + 32 * (li // 2 + 1))
                        for oc in range(2):
                            for dcc in range(8):
                                st = (gdc == 0 and dcc == 0)
                                sp = (gdc == 3 and dcc == 7)
                                rhs = stream[:, (gdc * 8 + dcc) * 1024 +
                                             half * 512:
                                             (gdc * 8 + dcc) * 1024 +
                                             half * 512 + 512]
                                t.matmul(out=ps_a[oc][:, :],
                                         lhsT=w13bufs[li % 2][
                                             :, dcc * 256 + oc * 128:
                                             dcc * 256 + oc * 128 + 128],
                                         rhs=rhs, start=st, stop=sp)
                                mm = t.matmul(
                                    out=ps_b[oc][:, :],
                                    lhsT=w13bufs[li % 2][
                                        :, 2048 + dcc * 256 + oc * 128:
                                        2048 + dcc * 256 + oc * 128 + 128],
                                    rhs=rhs, start=st, stop=sp)
                        mm.then_inc(p.sems[s1G], 1)

            block.tensor(t1)
            p.cnt[s1G] += 43 * 4

            def a1(a):
                for ob in range(43):
                    a.wait_ge(p.sems[s1G], base[s1G] + ob * 4 + 4)
                    for oc in range(2):
                        if ob >= 1:
                            a.wait_ge(p.sems[s1H], base[s1H] + (ob - 1) * 2 + oc + 1)
                        a.activation(out=siluA[oc][:, :], in_=ps_a[oc][:, :],
                                     func=AF.Silu).then_inc(p.sems[s1E], 1)

            block.scalar(a1)
            p.cnt[s1E] += 43 * 2

            def v1(v):
                for ob in range(43):
                    for oc in range(2):
                        v.wait_ge(p.sems[s1E], base[s1E] + ob * 2 + oc + 1)
                        occ = ob * 2 + oc
                        v.tensor_tensor(
                            out=h_sb[:, occ * 512:(occ + 1) * 512],
                            in0=siluA[oc][:, :], in1=ps_b[oc][:, :],
                            op=ALU.mult).then_inc(p.sems[s1H], 1)

            block.vector(v1)
            p.cnt[s1H] += 43 * 2

        with contextlib.ExitStack() as ph:
            sq_sb = [ph.enter_context(
                nc.sbuf_tensor(f"fsq{tg}{i}", [128, 512], BF16))
                for i in range(4)]
            ps_f = [ph.enter_context(
                nc.psum_tensor(f"psf{tg}{i}", [128, 512], F32))
                for i in range(4)]
            ps_sz = ph.enter_context(nc.psum_tensor(f"fsz{tg}", [1, 512], F32))
            ps_sq2 = ph.enter_context(nc.psum_tensor(f"fsq2{tg}", [1, 512], F32))

            OGN = [8] * 10 + [6]

            def g2(g):
                g.wait_ge(p.sems[s1G], base[s1G] + 172)
                for sup in range(8):
                    for og in range(11):
                        lj = sup * 11 + og
                        if lj >= 2:
                            g.wait_ge(p.sems[s2G], base[s2G] + lj - 1)
                        g.dma_start(
                            out=w2bufs[lj % 2][:, 0:OGN[og] * 512].rearrange(
                                "p (dc c) -> p dc c", dc=OGN[og]),
                            in_=w2t[og * 1024:og * 1024 + OGN[og] * 128,
                                    sup * 512:(sup + 1) * 512].rearrange(
                                "(dc p) o -> p dc o", p=128),
                        ).then_inc(p.sems[s2Ls[lj % 2]], 16)

            block.sync(g2)
            p.cnt[s2L] += 16 * 44
            p.cnt[s2L1] += 16 * 44

            def t2(t):
                t.wait_ge(p.sems[s1H], base[s1H] + 86)
                for sup in range(8):
                    if sup >= 1:
                        t.wait_ge(p.sems[sZE], base[sZE] + (sup - 1) * 4 + 4)
                    for og in range(11):
                        lj = sup * 11 + og
                        t.wait_ge(p.sems[s2Ls[lj % 2]],
                                  base[s2Ls[lj % 2]] + 16 * (lj // 2 + 1))
                        t.wait_ge(p.sems[s1H], base[s1H] +
                                  min(og * 8 + OGN[og], 86))
                        for ocg in range(OGN[og]):
                            oc = og * 8 + ocg
                            for qc in range(4):
                                mm = t.matmul(
                                    out=ps_f[qc][:, :],
                                    lhsT=w2bufs[lj % 2][
                                        :, ocg * 512 + qc * 128:
                                        ocg * 512 + qc * 128 + 128],
                                    rhs=h_sb[:, oc * 512:(oc + 1) * 512],
                                    start=(oc == 0), stop=(oc == 85))
                        mm.then_inc(p.sems[s2G], 1)
                    if sup >= 1:
                        for qc in range(4):
                            kk = (sup - 1) * 4 + qc
                            t.wait_ge(p.sems[sZE], base[sZE] + kk + 1)
                            t.wait_ge(p.sems[sSQ], base[sSQ] + kk + 1)
                            dpc = kk
                            sl = stream[:, dpc * 1024 + half * 512:
                                        dpc * 1024 + half * 512 + 512]
                            t.matmul(out=ps_sz[:, :], lhsT=ones_col[:, :],
                                     rhs=sl, start=(kk == 0), stop=False)
                            t.matmul(out=ps_sq2[:, :], lhsT=ones_col[:, :],
                                     rhs=sq_sb[qc][:, :], start=(kk == 0),
                                     stop=False).then_inc(p.sems[sZS], 1)
                for qc in range(4):
                    kk = 7 * 4 + qc
                    t.wait_ge(p.sems[sZE], base[sZE] + kk + 1)
                    t.wait_ge(p.sems[sSQ], base[sSQ] + kk + 1)
                    sl = stream[:, kk * 1024 + half * 512:
                                kk * 1024 + half * 512 + 512]
                    t.matmul(out=ps_sz[:, :], lhsT=ones_col[:, :],
                             rhs=sl, start=False, stop=(qc == 3))
                    t.matmul(out=ps_sq2[:, :], lhsT=ones_col[:, :],
                             rhs=sq_sb[qc][:, :], start=False,
                             stop=(qc == 3)).then_inc(p.sems[sZS], 1)

            block.tensor(t2)
            p.cnt[s2G] += 8 * 11
            p.cnt[sZS] += 32

            def v2(v):
                for sup in range(8):
                    for qc in range(4):
                        v.wait_ge(p.sems[s2G], base[s2G] + sup * 11 + 11)
                        dpc = sup * 4 + qc
                        sl = stream[:, dpc * 1024 + half * 512:
                                    dpc * 1024 + half * 512 + 512]
                        v.tensor_tensor(out=sl, in0=ps_f[qc][:, :], in1=sl,
                                        op=ALU.add).then_inc(p.sems[sZE], 1)

            block.vector(v2)
            p.cnt[sZE] += 32

            def a2(a):
                for sup in range(8):
                    for qc in range(4):
                        kk = sup * 4 + qc
                        a.wait_ge(p.sems[sZE], base[sZE] + kk + 1)
                        if kk >= 4:
                            a.wait_ge(p.sems[sZS], base[sZS] + kk - 3)
                        dpc = kk
                        sl = stream[:, dpc * 1024 + half * 512:
                                    dpc * 1024 + half * 512 + 512]
                        a.activation(out=sq_sb[qc][:, :], in_=sl,
                                     func=AF.Square).then_inc(p.sems[sSQ], 1)

            block.scalar(a2)
            p.cnt[sSQ] += 32

            dn = emit_ln_finalize(
                p, block, ps_sz=ps_sz, ps_sq=ps_sq2, stream=stream, half=half,
                ones_row=ones_row, tag=tg + "L",
                stats_wait=(sZS, base[sZS] + 32), ln_scratch=ln_scratch)
            dones.append(dn)
            prev_f2 = (s2G, p.cnt[s2G])
    return dones


def emit_l1kv(p, block, *, stream, wqkv_dram, kout, vout, x1out, wbuf, ident,
              tag="k1"):
    nc = p.nc
    sL, sP, sV, sT, sST = (x + tag for x in ("l", "p", "v", "t", "s"))
    for s in (sL, sP, sV, sT, sST):
        p.sem(s)
    base = {s: p.cnt[s] for s in (sL, sP, sV, sT, sST)}
    wr = wqkv_dram.rearrange("(dc p) n -> p dc n", p=128)

    with contextlib.ExitStack() as ph:
        k_own = ph.enter_context(nc.sbuf_tensor("kown", [128, 1024], BF16))
        v_own = ph.enter_context(nc.sbuf_tensor("vown", [128, 1024], BF16))
        vt_own = ph.enter_context(nc.sbuf_tensor("vtown", [128, 1024], BF16))
        ps_kv = [ph.enter_context(nc.psum_tensor(f"k1p{i}", [128, 512], F32))
                 for i in range(4)]
        ps_t = ph.enter_context(nc.psum_tensor("k1t", [128, 128], BF16))

        def g_body(g):
            g.dma_start(out=wbuf[:, :],
                        in_=wr[:, :, D:D + 256]).then_inc(p.sems[sL], 16)

        block.gpsimd(g_body)
        p.cnt[sL] += 16

        def t_body(t):
            t.wait_ge(p.sems[sL], base[sL] + 16)
            for half in range(2):
                for kv in range(2):
                    for dc in range(NKC):
                        mm = t.matmul(
                            out=ps_kv[half * 2 + kv][:, :],
                            lhsT=wbuf[:, dc * 256 + kv * 128:
                                      dc * 256 + kv * 128 + 128],
                            rhs=stream[:, dc * 1024 + half * 512:
                                       dc * 1024 + half * 512 + 512],
                            start=(dc == 0), stop=(dc == NKC - 1))
                    mm.then_inc(p.sems[sP], 1)
            for tc in range(8):
                t.wait_ge(p.sems[sV], base[sV] + 4)
                if tc >= 1:
                    t.wait_ge(p.sems[sV], base[sV] + 4 + tc)
                t.transpose(out=ps_t[:, :],
                            in_=v_own[:, tc * 128:(tc + 1) * 128],
                            identity=ident[:, :]).then_inc(p.sems[sT], 1)

        block.tensor(t_body)
        p.cnt[sP] += 4
        p.cnt[sT] += 8

        def v_body(v):
            for half in range(2):
                for kv in range(2):
                    j = half * 2 + kv
                    v.wait_ge(p.sems[sP], base[sP] + j + 1)
                    dst = k_own if kv == 0 else v_own
                    v.tensor_copy(out=dst[:, half * 512:half * 512 + 512],
                                  in_=ps_kv[j][:, :]).then_inc(p.sems[sV], 1)
            for tc in range(8):
                v.wait_ge(p.sems[sT], base[sT] + tc + 1)
                v.tensor_copy(out=vt_own[:, tc * 128:(tc + 1) * 128],
                              in_=ps_t[:, :]).then_inc(p.sems[sV], 1)

        block.vector(v_body)
        p.cnt[sV] += 12

        def s_body(sy):
            sy.wait_ge(p.sems[sV], base[sV] + 4)
            sy.dma_start(out=kout[:, :], in_=k_own[:, :]).then_inc(p.sems[sST], 16)
            sy.wait_ge(p.sems[sV], base[sV] + 12)
            for tc in range(8):
                sy.dma_start(out=vout[tc * 128:(tc + 1) * 128, :],
                             in_=vt_own[:, tc * 128:(tc + 1) * 128]).then_inc(
                    p.sems[sST], 16)
            for dc in range(NKC):
                sy.dma_start(out=x1out[dc * 128:(dc + 1) * 128, :],
                             in_=stream[:, dc * 1024:(dc + 1) * 1024]).then_inc(
                    p.sems[sST], 16)
            sy.wait_ge(p.sems[sST], base[sST] + 16 * (1 + 8 + NKC))

        block.sync(s_body)
        p.cnt[sST] += 16 * (1 + 8 + NKC)
    return (sST, p.cnt[sST])


def build_main1(upto="full"):
    nc = bass.Bass("TRN2", target_bir_lowering=False)
    x0t_own = nc.dram_tensor("x0t_own", [D, TPC], F32, kind="ExternalInput").ap()
    x0t_batch = nc.dram_tensor("x0t_batch", [D, S], F32, kind="ExternalInput").ap()
    wq = nc.dram_tensor("wq", [D, D], F32, kind="ExternalInput").ap()
    wkv = nc.dram_tensor("wkv", [D, 256], F32, kind="ExternalInput").ap()
    wo = nc.dram_tensor("wo", [D, D], F32, kind="ExternalInput").ap()
    w1t = nc.dram_tensor("w1t", [D, HID], BF16, kind="ExternalInput").ap()
    w3t = nc.dram_tensor("w3t", [D, HID], BF16, kind="ExternalInput").ap()
    w2t = nc.dram_tensor("w2t", [HID, D], BF16, kind="ExternalInput").ap()
    wqkv = nc.dram_tensor("wqkv", [D, 4352], F32, kind="ExternalInput").ap()
    x1out = nc.dram_tensor("x1out", [D, TPC], BF16, kind="ExternalOutput").ap()
    kout = nc.dram_tensor("kout", [128, TPC], BF16, kind="ExternalOutput").ap()
    vout = nc.dram_tensor("vout", [TPC, 128], BF16, kind="ExternalOutput").ap()
    if upto != "full":
        dbg_kf = nc.dram_tensor("dbg_kf", [128, S], BF16,
                                kind="ExternalOutput").ap()
        dbg_vt = nc.dram_tensor("dbg_vt", [128, S], BF16,
                                kind="ExternalOutput").ap()
        dbg_ao = nc.dram_tensor("dbg_ao", [128, 32768], BF16,
                                kind="ExternalOutput").ap()
        dbg_st = nc.dram_tensor("dbg_st", [128, 32768], BF16,
                                kind="ExternalOutput").ap()

    with contextlib.ExitStack() as es:
        p = Prog(nc, es)
        stream_t = p.sb("stream", [128, NKC * 1024], BF16)
        A_t = p.sb("bigA", [128, NOC * 512], BF16)
        dm_t = p.sb("dmf", [128, JW], F32)
        mh_t = p.sb("masterh", [128, JW], BF16)
        wb_t = p.sb("Wb", [128, 8192], BF16)
        ones_col = p.sb("ones_col", [128, 1], BF16)
        ones_row = p.sb("ones_row", [1, 128], BF16)
        ident = p.sb("ident", [128, 128], BF16)
        block = es.enter_context(nc.Block())

        stream = stream_t[:, :]
        A = A_t[:, :]
        dm = dm_t[:, :]
        masterh = mh_t[:, :]
        attn_out = A[:, 0:32768]
        kf = A[:, 32768:36864]
        vtm = A[:, 36864:40960]
        h_sb = A
        wq_bufs = [wb_t[:, 0:4096], wb_t[:, 4096:8192]]
        w13bufs = [masterh[:, 0:4096], dm.bitcast(BF16)[:, 0:4096]]
        w2bufs = w13bufs
        tmp_i16 = A.bitcast(I16)[:, 0:JW]
        tmpf = A.bitcast(F32)[:, JW // 2: JW // 2 + JW]
        ln_scratch = dict(
            murow=dm[0:1, 2048:2560],
            msrow=dm[0:1, 2560:3072],
            varrow=dm[0:1, 3072:3584],
            mu_bf=masterh[0:1, 4096:4608],
            rstd_bf=dm.bitcast(BF16)[0:1, 8704:9216],
            mu_rep=dm.bitcast(BF16)[:, 7168:7680],
            rstd_rep=dm.bitcast(BF16)[:, 7680:8192],
        )

        def dump():
            p.sem("dbgS")

            def s_body(sy):
                for dc in range(NKC):
                    sy.dma_start(out=dbg_st[:, dc * 1024:(dc + 1) * 1024],
                                 in_=stream[:, dc * 1024:(dc + 1) * 1024]
                                 ).then_inc(p.sems["dbgS"], 16)
                sy.dma_start(out=dbg_kf[:, :], in_=kf).then_inc(p.sems["dbgS"], 16)
                sy.dma_start(out=dbg_vt[:, :], in_=vtm).then_inc(p.sems["dbgS"], 16)
                for q in range(32):
                    sy.dma_start(out=dbg_ao[:, q * 1024:(q + 1) * 1024],
                                 in_=attn_out[:, q * 1024:(q + 1) * 1024]
                                 ).then_inc(p.sems["dbgS"], 16)
                sy.wait_ge(p.sems["dbgS"], 16 * 66)

            block.sync(s_body)

        fin = emit_init_phase(p, block, ones_col, ones_row, ident, dm,
                              tmp_i16, tmpf)
        emit_fence(p, block, [fin])
        emit_stream_load(p, block, stream, x0t_own)
        emit_l0kv(p, block, x0t_batch=x0t_batch, wkv_dram=wkv, kf=kf, vtm=vtm,
                  ident=ident, wbuf=wb_t[:, :])
        emit_fence(p, block, [("xl", p.cnt["xl"]), ("kvv", p.cnt["kvv"])])
        if upto == "kv":
            dump()
            return nc
        fatt = emit_attention(p, block, stream=stream, kf=kf, vtm=vtm,
                              attn_out=attn_out, dm=dm, masterh=masterh,
                              ones_col=ones_col, ones_row=ones_row,
                              wq_dram=wq, wq_col0=0, wq_sb=wq_bufs, tag="a0")
        emit_fence(p, block, [fatt])
        if upto == "attn":
            dump()
            return nc
        lns = emit_outproj_ln(p, block, stream=stream, rhs=attn_out,
                              w_dram=wo, w_col0=0, wbufs=wq_bufs,
                              ones_col=ones_col, ones_row=ones_row, tag="o0",
                              ln_scratch=ln_scratch)
        emit_fence(p, block, lns)
        if upto == "wo":
            dump()
            return nc
        lns2 = emit_ffn(p, block, stream=stream, h_sb=h_sb, w1t=w1t, w3t=w3t,
                        w2t=w2t, ones_col=ones_col, ones_row=ones_row,
                        w13bufs=w13bufs, w2bufs=w2bufs, tag="f0",
                        ln_scratch=ln_scratch)
        emit_fence(p, block, lns2)
        if upto == "ffn":
            dump()
            return nc
        emit_l1kv(p, block, stream=stream, wqkv_dram=wqkv, kout=kout,
                  vout=vout, x1out=x1out, wbuf=wb_t[:, :], ident=ident)

    return nc


_MAIN1_NC = None


def _x_transposed_slices(x):
    outs = []
    xt = [np.ascontiguousarray(np.asarray(x[b]).T) for b in range(B)]
    for c in range(NCORES):
        b, r = c // 4, c % 4
        outs.append((np.ascontiguousarray(xt[b][:, r * TPC:(r + 1) * TPC]),
                     xt[b]))
    return outs


def run_main1(inputs, w1t, w3t, w2t, upto="full"):
    global _MAIN1_NC
    if _MAIN1_NC is None:
        _MAIN1_NC = build_main1(upto)
    xs = _x_transposed_slices(inputs["L0_attn_hidden_states"])
    wq = np.asarray(inputs["L0_attn_Wq"])
    wkv = np.asarray(inputs["L0_attn_Wkv"])
    wo = np.asarray(inputs["L0_attn_Wo"])
    wqkv = np.asarray(inputs["L1_attn_w_qkv"])
    maps = []
    for c in range(NCORES):
        maps.append(dict(x0t_own=xs[c][0], x0t_batch=xs[c][1], wq=wq,
                         wkv=wkv, wo=wo, w1t=w1t, w3t=w3t, w2t=w2t,
                         wqkv=wqkv))
    res = run_bass_kernel_spmd(_MAIN1_NC, maps, core_ids=list(range(NCORES)))
    return res


def emit_kv_load(p, block, *, kf, vtm, kf_in, vt_in):
    s = "kvl"
    p.sem(s)

    def g_body(g):
        g.dma_start(out=kf, in_=kf_in[:, :]).then_inc(p.sems[s], 16)
        for tc in range(32):
            g.dma_start(out=vtm[:, tc * 128:(tc + 1) * 128],
                        in_=vt_in[tc * 128:(tc + 1) * 128, :]).then_inc(
                p.sems[s], 16)

    block.gpsimd(g_body)
    p.cnt[s] += 16 * 33
    return (s, p.cnt[s])


def emit_stream_copy(p, block, *, stream, xcopy):
    s = "xc"
    p.sem(s)

    def v_body(v):
        for dc in range(NKC):
            mm = v.tensor_copy(out=xcopy[:, dc * 1024:(dc + 1) * 1024],
                               in_=stream[:, dc * 1024:(dc + 1) * 1024])
        mm.then_inc(p.sems[s], 1)

    block.vector(v_body)
    p.cnt[s] += 1
    return (s, p.cnt[s])


def emit_lora_prep(p, block, *, gate_row, lora_A, lora_B, asha, bsha,
                   A_effT, B_effT, A_shT, B_shT, acc_f32, abuf_bf, bacc_f32,
                   bbuf_bf, ones_row, ident, tag="lp", stage="full",
                   gsb_dbg=None):
    nc = p.nc
    sL, sB_, sV, sT = (x + tag for x in ("l", "b", "v", "t"))
    for s in (sL, sB_, sV, sT):
        p.sem(s)
    p.sem("tp" + tag)
    base = {s: p.cnt[s] for s in (sL, sB_, sV, sT)}
    tp0 = p.cnt["tp" + tag]

    with contextlib.ExitStack() as ph:
        grow = ph.enter_context(nc.sbuf_tensor("grow" + tag, [1, 16], BF16))
        gsb = ph.enter_context(nc.sbuf_tensor("gsb" + tag, [128, 16], F32))
        ps_g = ph.enter_context(nc.psum_tensor("psg" + tag, [128, 16], F32))
        ps_t = [ph.enter_context(nc.psum_tensor(f"pst{tag}{i}", [128, 128], BF16))
                for i in range(2)]

        def g_body(g):
            g.dma_start(out=grow[:, :], in_=gate_row[:, :]).then_inc(p.sems[sL], 16)
            for n in range(16):
                if n >= 1:
                    g.wait_ge(p.sems[sV], base[sV] + n + 1)
                g.dma_start(out=abuf_bf[0:32, :],
                            in_=lora_A[n, :, :]).then_inc(p.sems[sL], 16)
            for n in range(16):
                if n >= 1:
                    g.wait_ge(p.sems[sV], base[sV] + n + 18)
                g.dma_start(
                    out=bbuf_bf[:, :].rearrange("p (dc r) -> p dc r", r=32),
                    in_=lora_B[n, :, :].rearrange("(dc p) r -> p dc r", p=128),
                ).then_inc(p.sems[sL], 16)
            if stage in ("acc", "tpa"):
                return
            g.wait_ge(p.sems["tp" + tag], tp0 + 32)
            g.dma_start(out=abuf_bf[0:32, :], in_=asha[:, :]).then_inc(
                p.sems[sL], 16)
            g.wait_ge(p.sems["tp" + tag], tp0 + 64)
            g.dma_start(
                out=bbuf_bf[:, :].rearrange("p (dc r) -> p dc r", r=32),
                in_=bsha[:, :].rearrange("(dc p) r -> p dc r", p=128),
            ).then_inc(p.sems[sL], 16)

        block.gpsimd(g_body)
        p.cnt[sL] += 16 * 35

        def t_body(t):
            t.wait_ge(p.sems[sL], base[sL] + 16)
            t.matmul(out=ps_g[:, :], lhsT=ones_row[:, :], rhs=grow[:, :],
                     start=True, stop=True).then_inc(p.sems[sB_], 1)
            if stage == "acc":
                return
            for grp in range(2 if stage == "tpa" else 4):
                t.wait_ge(p.sems[sV], base[sV] + [18, 35, 36, 37][grp])
                src = [abuf_bf, bbuf_bf, abuf_bf, bbuf_bf][grp]
                for dc in range(NKC):
                    gi = grp * 32 + dc
                    if gi >= 2:
                        t.wait_ge(p.sems[sT], base[sT] + gi - 1)
                    if grp in (0, 2):
                        in_ = src[0:32, dc * 128:(dc + 1) * 128]
                        out = ps_t[gi % 2][0:128, 0:32]
                        idn = ident[0:32, 0:32]
                    else:
                        in_ = src[:, dc * 32:(dc + 1) * 32]
                        out = ps_t[gi % 2][0:32, 0:128]
                        idn = ident[:, :]
                    t.transpose(out=out, in_=in_,
                                identity=idn).then_inc(p.sems["tp" + tag], 1)

        tpbase = tp0
        block.tensor(t_body)
        p.cnt[sB_] += 1
        p.cnt["tp" + tag] += 128

        def v_body(v):
            v.wait_ge(p.sems[sB_], base[sB_] + 1)
            v.tensor_copy(out=gsb[:, :], in_=ps_g[:, :]).then_inc(p.sems[sV], 1)
            v.wait_ge(p.sems[sV], base[sV] + 1)
            if gsb_dbg is not None:
                v.tensor_copy(out=gsb_dbg, in_=gsb[:, :])
            for n in range(16):
                v.wait_ge(p.sems[sL], base[sL] + 16 * (n + 2))
                if n == 0:
                    v.tensor_scalar(out=acc_f32[0:32, :], in0=abuf_bf[0:32, :],
                                    scalar1=gsb[0:32, 0:1], scalar2=None,
                                    op0=ALU.mult).then_inc(p.sems[sV], 1)
                else:
                    v.scalar_tensor_tensor(
                        out=acc_f32[0:32, :], in0=abuf_bf[0:32, :],
                        scalar=gsb[0:32, n:n + 1], in1=acc_f32[0:32, :],
                        op0=ALU.mult, op1=ALU.add).then_inc(p.sems[sV], 1)
            v.tensor_copy(out=abuf_bf[0:32, :], in_=acc_f32[0:32, :]).then_inc(
                p.sems[sV], 1)
            for n in range(16):
                v.wait_ge(p.sems[sL], base[sL] + 16 * (n + 18))
                if n == 0:
                    v.tensor_scalar(out=bacc_f32[:, :], in0=bbuf_bf[:, :],
                                    scalar1=gsb[:, 0:1], scalar2=None,
                                    op0=ALU.mult).then_inc(p.sems[sV], 1)
                else:
                    v.scalar_tensor_tensor(
                        out=bacc_f32[:, :], in0=bbuf_bf[:, :],
                        scalar=gsb[:, n:n + 1], in1=bacc_f32[:, :],
                        op0=ALU.mult, op1=ALU.add).then_inc(p.sems[sV], 1)
            v.tensor_copy(out=bbuf_bf[:, :], in_=bacc_f32[:, :]).then_inc(
                p.sems[sV], 1)
            if stage == "acc":
                return
            for grp in range(2 if stage == "tpa" else 4):
                if grp == 2:
                    v.wait_ge(p.sems[sL], base[sL] + 16 * 34)
                    v.nop().then_inc(p.sems[sV], 1)
                    v.wait_ge(p.sems[sL], base[sL] + 16 * 35)
                    v.nop().then_inc(p.sems[sV], 1)
                dstv = [A_effT, B_effT, A_shT, B_shT][grp]
                for dc in range(NKC):
                    gi = grp * 32 + dc
                    v.wait_ge(p.sems["tp" + tag], tpbase + gi + 1)
                    if grp in (0, 2):
                        v.tensor_copy(out=dstv[:, dc * 32:(dc + 1) * 32],
                                      in_=ps_t[gi % 2][0:128, 0:32]).then_inc(
                            p.sems[sT], 1)
                    else:
                        v.tensor_copy(out=dstv[0:32, dc * 128:(dc + 1) * 128],
                                      in_=ps_t[gi % 2][0:32, 0:128]).then_inc(
                            p.sems[sT], 1)

        block.vector(v_body)
        if stage == "acc":
            p.cnt[sV] += 35
            return (sV, p.cnt[sV])
        if stage == "tpa":
            p.cnt[sV] += 35
            p.cnt[sT] += 64
            return (sT, p.cnt[sT])
        p.cnt[sV] += 37
        p.cnt[sT] += 128
    return (sT, p.cnt[sT])


def emit_output(p, block, *, stream, out_dram, otiles, ident, tag="ot"):
    nc = p.nc
    sT, sV, sS = (x + tag for x in ("t", "v", "s"))
    for s in (sT, sV, sS):
        p.sem(s)
    base = {s: p.cnt[s] for s in (sT, sV, sS)}

    with contextlib.ExitStack() as ph:
        ps_t = [ph.enter_context(nc.psum_tensor(f"oto{tag}{i}", [128, 128], BF16))
                for i in range(2)]

        def t_body(t):
            for tt in range(8):
                for dc in range(NKC):
                    gi = tt * NKC + dc
                    if gi >= 2:
                        t.wait_ge(p.sems[sV], base[sV] + gi - 1)
                    t.transpose(
                        out=ps_t[gi % 2][:, :],
                        in_=stream[:, dc * 1024 + tt * 128:
                                   dc * 1024 + tt * 128 + 128],
                        identity=ident[:, :]).then_inc(p.sems[sT], 1)

        block.tensor(t_body)
        p.cnt[sT] += 8 * NKC

        def v_body(v):
            for tt in range(8):
                if tt >= 2:
                    v.wait_ge(p.sems[sS], base[sS] + 16 * (tt - 1))
                for dc in range(NKC):
                    gi = tt * NKC + dc
                    v.wait_ge(p.sems[sT], base[sT] + gi + 1)
                    v.tensor_copy(
                        out=otiles[tt % 2][:, dc * 128:(dc + 1) * 128],
                        in_=ps_t[gi % 2][:, :]).then_inc(p.sems[sV], 1)

        block.vector(v_body)
        p.cnt[sV] += 8 * NKC

        def s_body(sy):
            for tt in range(8):
                sy.wait_ge(p.sems[sV], base[sV] + NKC * (tt + 1))
                sy.dma_start(out=out_dram[tt * 128:(tt + 1) * 128, :],
                             in_=otiles[tt % 2][:, :]).then_inc(p.sems[sS], 16)
            sy.wait_ge(p.sems[sS], base[sS] + 16 * 8)

        block.sync(s_body)
        p.cnt[sS] += 16 * 8
    return (sS, p.cnt[sS])


def build_main2(upto="full"):
    nc = bass.Bass("TRN2", target_bir_lowering=False)
    x1t_own = nc.dram_tensor("x1t_own", [D, TPC], BF16, kind="ExternalInput").ap()
    kf_in = nc.dram_tensor("kf_in", [128, S], BF16, kind="ExternalInput").ap()
    vt_in = nc.dram_tensor("vt_in", [S, 128], BF16, kind="ExternalInput").ap()
    wqkv = nc.dram_tensor("wqkv", [D, 4352], F32, kind="ExternalInput").ap()
    wc = nc.dram_tensor("wc", [D, D], F32, kind="ExternalInput").ap()
    kern = nc.dram_tensor("kern", [D, D], F32, kind="ExternalInput").ap()
    bias32 = nc.dram_tensor("bias32", [128, 32], F32, kind="ExternalInput").ap()
    gate_row = nc.dram_tensor("gate_row", [1, 16], F32, kind="ExternalInput").ap()
    lora_A = nc.dram_tensor("lora_A", [16, 32, D], F32, kind="ExternalInput").ap()
    lora_B = nc.dram_tensor("lora_B", [16, D, 32], F32, kind="ExternalInput").ap()
    asha = nc.dram_tensor("asha", [32, D], F32, kind="ExternalInput").ap()
    bsha = nc.dram_tensor("bsha", [D, 32], F32, kind="ExternalInput").ap()
    out_tm = nc.dram_tensor("out_tm", [TPC, D], F32, kind="ExternalOutput").ap()
    if upto != "full":
        dbg_st = nc.dram_tensor("dbg_st", [128, 32768], BF16,
                                kind="ExternalOutput").ap()
        dbg_f = nc.dram_tensor("dbg_f", [128, 8192], F32,
                               kind="ExternalOutput").ap()

    with contextlib.ExitStack() as es:
        p = Prog(nc, es)
        stream_t = p.sb("stream", [128, NKC * 1024], BF16)
        A_t = p.sb("bigA", [128, NOC * 512], BF16)
        dm_t = p.sb("dmf", [128, JW], F32)
        mh_t = p.sb("masterh", [128, JW], BF16)
        wb_t = p.sb("Wb", [128, 8192], BF16)
        ones_col = p.sb("ones_col", [128, 1], BF16)
        ones_row = p.sb("ones_row", [1, 128], BF16)
        ident = p.sb("ident", [128, 128], BF16)
        bias_sb = p.sb("bias_sb", [128, 32], F32)
        block = es.enter_context(nc.Block())

        stream = stream_t[:, :]
        A = A_t[:, :]
        dm = dm_t[:, :]
        masterh = mh_t[:, :]
        attn_out = A[:, 0:32768]
        kf = A[:, 32768:36864]
        vtm = A[:, 36864:40960]
        xcopy = A[:, 0:32768]
        A_effT = A[:, 32768:33792]
        B_effT = A[0:32, 33792:37888]
        A_shT = A[:, 37888:38912]
        B_shT = A[0:32, 38912:43008]
        wq_bufs = [wb_t[:, 0:4096], wb_t[:, 4096:8192]]
        tmp_i16 = A.bitcast(I16)[:, 0:JW]
        tmpf = A.bitcast(F32)[:, JW // 2: JW // 2 + JW]
        ln_scratch = dict(
            murow=dm[0:1, 2048:2560],
            msrow=dm[0:1, 2560:3072],
            varrow=dm[0:1, 3072:3584],
            mu_bf=masterh[0:1, 4096:4608],
            rstd_bf=dm.bitcast(BF16)[0:1, 8704:9216],
            mu_rep=dm.bitcast(BF16)[:, 7168:7680],
            rstd_rep=dm.bitcast(BF16)[:, 7680:8192],
        )
        gsb_dbg = p.sb("gsbdbg", [128, 16], F32)[:, :] if upto == "ax" else None
        acc_f32 = dm[0:32, 0:4096]
        abuf_bf = masterh[0:32, 0:4096]
        bacc_f32 = wb_t[:, :].bitcast(F32)[:, 0:1024]
        bbuf_bf = wb_t[:, 4096:5120]
        otiles = [A.bitcast(F32)[:, 0:4096], A.bitcast(F32)[:, 4096:8192]]

        def dump(which):
            p.sem("dbgS")

            def s_body(sy):
                src_ = stream if which == "stream" else attn_out
                for dc in range(NKC):
                    sy.dma_start(out=dbg_st[:, dc * 1024:(dc + 1) * 1024],
                                 in_=src_[:, dc * 1024:(dc + 1) * 1024]
                                 ).then_inc(p.sems["dbgS"], 16)
                sy.wait_ge(p.sems["dbgS"], 16 * NKC)

            block.sync(s_body)

        fin = emit_init_phase(p, block, ones_col, ones_row, ident, dm,
                              tmp_i16, tmpf)
        emit_fence(p, block, [fin])
        emit_stream_load(p, block, stream, x1t_own)
        fkv = emit_kv_load(p, block, kf=kf, vtm=vtm, kf_in=kf_in, vt_in=vt_in)
        emit_fence(p, block, [("xl", p.cnt["xl"]), fkv])
        if upto == "load":
            dump("stream")
            return nc
        fatt = emit_attention(p, block, stream=stream, kf=kf, vtm=vtm,
                              attn_out=attn_out, dm=dm, masterh=masterh,
                              ones_col=ones_col, ones_row=ones_row,
                              wq_dram=wqkv, wq_col0=0, wq_sb=wq_bufs, tag="a1")
        emit_fence(p, block, [fatt])
        if upto == "attn":
            dump("attn")
            return nc
        lns = emit_outproj_ln(p, block, stream=stream, rhs=attn_out,
                              w_dram=wc, w_col0=0, wbufs=wq_bufs,
                              ones_col=ones_col, ones_row=ones_row, tag="o1",
                              ln_scratch=ln_scratch)
        emit_fence(p, block, lns)
        if upto == "wc":
            dump("stream")
            return nc
        fxc = emit_stream_copy(p, block, stream=stream, xcopy=xcopy)
        flp = emit_lora_prep(p, block, gate_row=gate_row, lora_A=lora_A,
                             lora_B=lora_B, asha=asha, bsha=bsha,
                             A_effT=A_effT, B_effT=B_effT, A_shT=A_shT,
                             B_shT=B_shT, acc_f32=acc_f32, abuf_bf=abuf_bf,
                             bacc_f32=bacc_f32, bbuf_bf=bbuf_bf,
                             ones_row=ones_row, ident=ident,
                             gsb_dbg=gsb_dbg)
        emit_fence(p, block, [fxc, flp])
        if upto == "prep":
            dump("stream")
            return nc

        p.sem("axp")
        ax_sb = [p.sb(f"axsb{i}", [32, 512], BF16) for i in range(2)]
        axs_sb = [p.sb(f"axssb{i}", [32, 512], BF16) for i in range(2)]
        with contextlib.ExitStack() as ph:
            ps_ax = [ph.enter_context(
                nc.psum_tensor(f"psax{i}", [32, 512], F32)) for i in range(4)]

            def t_ax(t):
                for half in range(2):
                    for sh in range(2):
                        lt = A_effT if sh == 0 else A_shT
                        for dc in range(NKC):
                            mm = t.matmul(
                                out=ps_ax[half * 2 + sh][:, :],
                                lhsT=lt[:, dc * 32:(dc + 1) * 32],
                                rhs=xcopy[:, dc * 1024 + half * 512:
                                          dc * 1024 + half * 512 + 512],
                                start=(dc == 0), stop=(dc == NKC - 1))
                        mm.then_inc(p.sems["axp"], 1)

            block.tensor(t_ax)
            p.cnt["axp"] += 4
            p.sem("axv")

            def v_ax(v):
                for half in range(2):
                    for sh in range(2):
                        j = half * 2 + sh
                        v.wait_ge(p.sems["axp"], j + 1)
                        dst = (ax_sb if sh == 0 else axs_sb)[half]
                        v.tensor_copy(out=dst[:, :],
                                      in_=ps_ax[j][:, :]).then_inc(
                            p.sems["axv"], 1)

            block.vector(v_ax)
            p.cnt["axv"] += 4
            emit_fence(p, block, [("axv", p.cnt["axv"])])
            if upto == "ax":
                p.sem("dbgS")

                def s_dbg(sy):
                    sy.dma_start(out=dbg_f[:, 0:4096],
                                 in_=dm[:, 0:4096]).then_inc(p.sems["dbgS"], 16)
                    sy.dma_start(out=dbg_f[:, 4096:4112],
                                 in_=gsb_dbg[:, :]).then_inc(p.sems["dbgS"], 16)
                    sy.dma_start(out=dbg_st[:, 6144:10240],
                                 in_=masterh[:, 0:4096]).then_inc(p.sems["dbgS"], 16)
                    sy.dma_start(out=dbg_st[:, 0:1024],
                                 in_=A_effT).then_inc(p.sems["dbgS"], 16)
                    sy.dma_start(out=dbg_st[0:32, 1024:5120],
                                 in_=B_effT).then_inc(p.sems["dbgS"], 16)
                    sy.dma_start(out=dbg_st[0:32, 5120:5632],
                                 in_=ax_sb[0][:, :]).then_inc(p.sems["dbgS"], 16)
                    sy.dma_start(out=dbg_st[0:32, 5632:6144],
                                 in_=ax_sb[1][:, :]).then_inc(p.sems["dbgS"], 16)
                    sy.wait_ge(p.sems["dbgS"], 112)

                block.sync(s_dbg)

        if upto == "ax":
            return nc
        if True:
            def lora_mms(t, dpc, half, ps):
                t.matmul(out=ps[:, :], lhsT=B_effT[:, dpc * 128:(dpc + 1) * 128],
                         rhs=ax_sb[half][:, :], start=False, stop=False)
                return t.matmul(out=ps[:, :],
                                lhsT=B_shT[:, dpc * 128:(dpc + 1) * 128],
                                rhs=axs_sb[half][:, :], start=False, stop=True)

            p.sem("bl")

            def g_b(g):
                g.dma_start(out=bias_sb[:, :], in_=bias32[:, :]).then_inc(
                    p.sems["bl"], 16)

            block.gpsimd(g_b)
            p.cnt["bl"] += 16
            emit_fence(p, block, [("bl", p.cnt["bl"])])

            lns2 = emit_outproj_ln(p, block, stream=stream, rhs=xcopy,
                                   w_dram=kern, w_col0=0, wbufs=wq_bufs,
                                   ones_col=ones_col, ones_row=ones_row,
                                   tag="k1", ln_scratch=ln_scratch,
                                   bias_sb=bias_sb[:, :], extra_mms=lora_mms)
            emit_fence(p, block, lns2)
        if upto == "ffn":
            dump("stream")
            return nc
        emit_output(p, block, stream=stream, out_dram=out_tm, otiles=otiles,
                    ident=ident)

    return nc


_MAIN2_NC = None


def run_main2(inputs, res1, upto="full"):
    global _MAIN2_NC
    if _MAIN2_NC is None:
        _MAIN2_NC = build_main2(upto)
    wqkv = np.asarray(inputs["L1_attn_w_qkv"])
    wc = np.asarray(inputs["L1_attn_w_c"])
    kern = np.asarray(inputs["L1_ffn_kernel"])
    bias32 = np.ascontiguousarray(
        np.asarray(inputs["L1_ffn_bias"]).reshape(32, 128).T)
    gate = np.asarray(inputs["L1_ffn_gate"])
    lora_A = np.asarray(inputs["L1_ffn_lora_A"])
    lora_B = np.asarray(inputs["L1_ffn_lora_B"])
    asha = np.asarray(inputs["L1_ffn_lora_A_shared"])
    bsha = np.asarray(inputs["L1_ffn_lora_B_shared"])
    kf_b = [np.concatenate([res1.results[b * 4 + r]["kout"] for r in range(4)],
                           axis=1) for b in range(B)]
    vt_b = [np.concatenate([res1.results[b * 4 + r]["vout"] for r in range(4)],
                           axis=0) for b in range(B)]
    maps = []
    for c in range(NCORES):
        b = c // 4
        maps.append(dict(x1t_own=res1.results[c]["x1out"], kf_in=kf_b[b],
                         vt_in=vt_b[b], wqkv=wqkv, wc=wc, kern=kern,
                         bias32=bias32, gate_row=gate[c:c + 1], lora_A=lora_A,
                         lora_B=lora_B, asha=asha, bsha=bsha))
    return run_bass_kernel_spmd(_MAIN2_NC, maps, core_ids=list(range(NCORES)))
